# version 10
# baseline (speedup 1.0000x reference)
import numpy as np
import ml_dtypes
from contextlib import ExitStack

import concourse.bass as bass
import concourse.mybir as mybir
from concourse.bass_utils import run_bass_kernel_spmd

F32 = mybir.dt.float32
BF16 = mybir.dt.bfloat16
AF = mybir.ActivationFunctionType
ALU = mybir.AluOpType
AX = mybir.AxisListType.X

S = 2048
EMB = 1024
INNER = 2048
NH = 4
DH = 512
KS = 4
FF = 4096
NE = 8
NEC = EMB // 128
NFC = INNER // 128
NFF = FF // 128
NTC = S // 128
NTB = S // 512
LN_EPS = 1e-5
EPS = 1e-6
BIGM = 1.0e30
RT5 = float(np.sqrt(LN_EPS))
LNRTDH = float(0.5 * np.log(DH))

ENGS = ("tensor", "vector", "scalar", "gpsimd", "sync")

_DEBUG_TAPS = False
_SCRATCH_OUT = False
_B1_SERIAL = False
_D_SERIAL = False
_D_ALL_TAGS = frozenset(
    {"pre", "ld", "cm", "ti", "nrm", "st", "tr", "wd", "cmb"})
_D_SER_TAGS = set()
_LAST_REC = None


def _prep(inputs):
    g = lambda k: np.ascontiguousarray(np.asarray(inputs[k]))
    x = g("L0_attn_x")
    assert x.shape == (8, S, EMB)

    def blockdiag(w):
        M = np.zeros((NFC, 32, 4, 32, 4), np.float32)
        W = w.reshape(NFC, 32, 4, 4)
        for p in range(32):
            M[:, p, :, p, :] = np.swapaxes(W[:, p], 1, 2)
        return M.reshape(NFC, 128, 128)

    mq = blockdiag(g("L0_attn_w_q"))
    mk = blockdiag(g("L0_attn_w_k"))
    mv = blockdiag(g("L0_attn_w_v"))
    mvT = np.ascontiguousarray(np.swapaxes(mv, 1, 2))

    shared = {
        "wupT": np.ascontiguousarray(
            g("L0_attn_w_up").T.reshape(8, 128, 32, 128)
            .transpose(2, 1, 0, 3).reshape(32 * 128, 1024)),
        "mq": np.ascontiguousarray(mq.reshape(NFC * 128, 128)),
        "mk_": np.ascontiguousarray(mk.reshape(NFC * 128, 128)),
        "mv": np.ascontiguousarray(mv.reshape(NFC * 128, 128)),
        "mvT": np.ascontiguousarray(mvT.reshape(NFC * 128, 128)),
        "wigfgT": np.ascontiguousarray(
            np.concatenate([g("L0_attn_w_ig"), g("L0_attn_w_fg")], axis=0).T
            .reshape(48, 128, 8).transpose(1, 0, 2).reshape(128, 48 * 8)),
        "big": np.ascontiguousarray(g("L0_attn_b_ig")[:, None]),
        "bfg": np.ascontiguousarray(g("L0_attn_b_fg")[:, None]),
        "convw": np.ascontiguousarray(
            g("L0_attn_conv_w")[:, 0, :].reshape(NFC, 128, KS)
            .transpose(1, 0, 2).reshape(128, NFC * KS)),
        "convb": np.ascontiguousarray(g("L0_attn_conv_b").reshape(NFC, 128).T),
        "skipc": np.ascontiguousarray(g("L0_attn_skip").reshape(NFC, 128).T),
        "lnw_row": np.ascontiguousarray(g("L0_attn_ln_w").reshape(1, NH * DH)),
        "wr_r": np.ascontiguousarray(
            g("L0_ffn_wr").reshape(NEC, 128, NE).transpose(1, 0, 2)
            .reshape(128, NEC * NE)),
        "br_row": np.ascontiguousarray(g("L0_ffn_br").reshape(1, NE)),
        "b1_r": np.ascontiguousarray(
            g("L0_ffn_b1").reshape(NE, NFF, 128).transpose(2, 0, 1)
            .reshape(128, NE * NFF)),
        "b2": g("L0_ffn_b2"),
        "wdT": np.ascontiguousarray(g("L0_attn_w_down").T),
        "w1": np.ascontiguousarray(
            g("L0_ffn_w1").reshape(NE, 8, 128, 32, 128)
            .transpose(0, 3, 2, 1, 4).reshape(NE * 32 * 128, 1024)),
        "w2": np.ascontiguousarray(
            g("L0_ffn_w2").reshape(NE, 16, 2, 128, 2, 512)
            .transpose(0, 4, 1, 3, 2, 5).reshape(NE * 2 * 16 * 128, 1024)),
    }
    r_ = np.arange(4)[:, None, None]
    lt = np.arange(128)[None, :, None]
    ls = np.arange(512)[None, None, :]
    shared["posmask"] = np.where(ls < lt + 128 * r_, BIGM, 0.0).astype(
        np.float32).reshape(4 * 128, 512)
    shared["ident_bf"] = np.eye(128, dtype=ml_dtypes.bfloat16)
    shared["ident32"] = np.eye(128, dtype=np.float32)
    shared["ones_row"] = np.ones((1, 128), np.float32)
    shared["ones_colbf"] = np.ones((128, 1), ml_dtypes.bfloat16)
    shards = [np.ascontiguousarray(x[c]) for c in range(8)]
    return shared, shards


class Sem:
    __slots__ = ("name", "n", "h", "owner")

    def __init__(self, name, owner=None):
        self.name, self.n, self.h, self.owner = name, 0, None, owner

    def tok(self):
        return (self, self.n)


class Rec:
    def __init__(self):
        self.streams = {e: [] for e in ENGS}
        self.sems = []
        self.marks = []
        self.vec_dummy = None
        self._narrow_prev = {}

    @staticmethod
    def _out_free(args, kw):
        if kw.get("accum_out") is not None:
            return 1
        if not args:
            return 1 << 30
        ap = args[0]
        try:
            pairs = list(ap.ap)
        except Exception:
            return 1 << 30
        n = 1
        for st, ct in pairs:
            n *= ct
        return n

    def sem(self, name, owner=None):
        s = Sem(name, owner)
        self.sems.append(s)
        return s

    gidx = 0

    def emit(self, eng, fn, *args, wait=(), inc=None, inck=1, **kw):
        wait = [w for w in wait if w is not None]
        if eng == "vector" and self.vec_dummy is not None and fn != "nop":
            if self._narrow_prev.get(eng):
                Rec.gidx += 1
                self.streams[eng].append(
                    ("tensor_copy", self.vec_dummy, {}, (), None, Rec.gidx))
                self._narrow_prev[eng] = False
        tok = None
        if inc is not None:
            inc.n += inck
            tok = (inc, inc.n)
        Rec.gidx += 1
        self.streams[eng].append(
            (fn, args, kw, tuple(wait),
             (inc, inck) if inc else None, Rec.gidx))
        if eng == "vector" and fn not in ("nop", "dma_start") \
                and self._out_free(args, kw) < 2048:
            self._narrow_prev[eng] = True
        return tok

    def wait(self, eng, waits):
        Rec.gidx += 1
        self.streams[eng].append(
            (None, (), {}, tuple(w for w in waits if w is not None), None,
             Rec.gidx))

    def dma(self, q, out_ap, in_ap, sem, wait=()):
        return self.emit(q, "dma_start", out_ap, in_ap, wait=wait,
                         inc=sem, inck=16)

    def barrier(self, P, queue_waits=()):
        qmap = {e: [] for e in ENGS}
        for e, s in queue_waits:
            qmap[e].append((s, s.n))
        target = P.n + len(ENGS)
        for e in ENGS:
            P.n += 1
            Rec.gidx += 1
            self.streams[e].append(("nop", (), {}, tuple(qmap[e]), (P, 1),
                                    Rec.gidx))
        for e in ENGS:
            self.wait(e, [(P, target)])
        self.marks.append({e: len(self.streams[e]) for e in ENGS})

    def replay(self, nc, upto=None, cut=None):
        def mk(eng_name):
            def body(eng):
                high = {}
                stream = self.streams[eng_name]
                if upto is not None and upto < len(self.marks):
                    stream = stream[:self.marks[upto][eng_name]]
                for fn, args, kw, waits, inc, gi in stream:
                    if cut is not None and gi > cut:
                        break
                    for (s, v) in waits:
                        if high.get(s, -1) >= v:
                            continue
                        eng.wait_ge(s.h, v)
                        high[s] = v
                    if fn is None:
                        continue
                    if eng_name == "scalar" and fn == "tensor_copy":
                        fn = "copy"
                    ins = getattr(eng, fn)(*args, **kw)
                    if inc is not None:
                        ins.then_inc(inc[0].h, inc[1])
            return body

        with nc.Block() as block:
            block.tensor(mk("tensor"))
            block.vector(mk("vector"))
            block.scalar(mk("scalar"))
            block.gpsimd(mk("gpsimd"))
            block.sync(mk("sync"))


def both(*toks):
    return [t for t in toks if t is not None]


def build_nc(upto=None, cut=None):
    nc = bass.Bass("TRN2", target_bir_lowering=False,
                   detect_race_conditions=False)
    r = Rec()

    def inp(name, shape, dt=F32):
        return nc.dram_tensor(name, list(shape), dt, kind="ExternalInput").ap()

    x_c = inp("x_c", (S, EMB))
    wupT = inp("wupT", (32 * 128, 1024))
    mq_d = inp("mq", (NFC * 128, 128))
    mk_d = inp("mk_", (NFC * 128, 128))
    mv_d = inp("mv", (NFC * 128, 128))
    mvT_d = inp("mvT", (NFC * 128, 128))
    wigfgT_d = inp("wigfgT", (128, 48 * 8))
    big_d = inp("big", (4, 1))
    bfg_d = inp("bfg", (4, 1))
    convw_d = inp("convw", (128, NFC * KS))
    convb_d = inp("convb", (128, NFC))
    skipc_d = inp("skipc", (128, NFC))
    lnw_d = inp("lnw_row", (1, NH * DH))
    wr_d = inp("wr_r", (128, NEC * NE))
    br_d = inp("br_row", (1, NE))
    b1_d = inp("b1_r", (128, NE * NFF))
    b2_d = inp("b2", (NE, EMB))
    wdT_d = inp("wdT", (INNER, EMB))
    w1_d = inp("w1", (NE * 32 * 128, 1024))
    w2_d = inp("w2", (NE * 2 * 16 * 128, 1024))
    posm_d = inp("posmask", (4 * 128, 512))
    identbf_d = inp("ident_bf", (128, 128), BF16)
    ident32_d = inp("ident32", (128, 128))
    onesr_d = inp("ones_row", (1, 128))
    onescb_d = inp("ones_colbf", (128, 1), BF16)

    out_d = nc.dram_tensor("out", [S, EMB], F32, kind="ExternalOutput").ap()

    def mkscr(name, shape, dt):
        kind = "ExternalOutput" if _SCRATCH_OUT else "Internal"
        return nc.dram_tensor(name, list(shape), dt, kind=kind).ap()

    xm_scr = mkscr("xm_scr", (INNER, S), BF16)
    xc_scr = mkscr("xc_scr", (INNER, S), F32)
    z_scr = mkscr("z_scr", (INNER, S), F32)
    q_scr = mkscr("q_scr", (INNER, S), BF16)
    k_scr = mkscr("k_scr", (INNER, S), BF16)
    v_scr = mkscr("v_scr", (NH * S, DH), BF16)
    g_scr = mkscr("g_scr", (S, EMB), BF16)
    rows_scr = mkscr("rows_scr", (24, 8192), F32)
    taps = {}
    if _DEBUG_TAPS:
        taps["xnfm"] = mkscr("dbg_xnfm", (128, NEC * S), BF16)
        taps["mask"] = mkscr("dbg_mask", (128, NTC * NE), F32)
        taps["ffn"] = mkscr("dbg_ffn", (S, EMB), F32)
        taps["s"] = mkscr("dbg_s", (S, EMB), F32)
        taps["hst"] = mkscr("dbg_hst", (16 * 128, INNER), BF16)
        taps["hts"] = mkscr("dbg_hts", (16 * 128, INNER), F32)
        taps["rows"] = mkscr("dbg_rows", (16, 8192), F32)

    qS = r.sem("qS", "sync")
    qG = r.sem("qG", "gpsimd")
    qA = r.sem("qA", "scalar")
    sT = r.sem("sT", "tensor")
    sV = r.sem("sV", "vector")
    sS = r.sem("sS", "scalar")
    P = r.sem("P", None)
    LB1 = [r.sem(f"LB1{i}", "sync") for i in range(2)]
    LW0 = [r.sem(f"LW0{i}", "sync") for i in range(2)]
    SXC = [r.sem(f"SXC{i}", "scalar") for i in range(2)]
    SZ2 = [r.sem(f"SZ2{i}", "scalar") for i in range(2)]
    SXM = [r.sem(f"SXM{i}", "gpsimd") for i in range(2)]
    LB3 = [r.sem(f"LB3{i}", "sync") for i in range(2)]
    LB3c = [r.sem(f"LB3c{i}", "sync") for i in range(2)]
    SQK = [r.sem(f"SQK{i}", "scalar") for i in range(2)]
    SV2 = r.sem("SV2", "sync")
    SIG = r.sem("SIG", "scalar")
    LW1 = [r.sem(f"LW1{i}", "sync") for i in range(2)]
    LW2 = [r.sem(f"LW2{i}", "sync") for i in range(4)]
    SG2 = r.sem("SG2", "gpsimd")


    esG = ExitStack()
    esAB3 = ExitStack()
    esB1C = ExitStack()
    esB3D = ExitStack()
    esB2o = ExitStack()
    esCo = ExitStack()

    def _sb(stk, name, p, f, dt):
        return stk.enter_context(nc.sbuf_tensor(name + "_s", [p, f], dt))

    def sbg(name, p, f, dt):
        return _sb(esG, name, p, f, dt)

    big_sb = sbg("big", 4, 1, F32)
    bfg_sb = sbg("bfg", 4, 1, F32)
    lnw_p0 = sbg("lnw_p0", 1, NH * DH, F32)
    br_p0 = sbg("br_p0", 1, NE, F32)
    onesr_sb = sbg("onesr", 1, 128, F32)
    onescb_sb = sbg("onescb", 128, 1, BF16)
    identbf_sb = sbg("identbf", 128, 128, BF16)
    ident32_sb = sbg("ident32", 128, 128, F32)
    epscol_sb = sbg("epscol", 128, 1, F32)
    rt5eps_sb = sbg("rt5eps", 128, 1, F32)
    onecol_sb = sbg("onecol", 128, 1, F32)
    djunk_sb = sbg("djunk", 128, 2, F32)
    r.vec_dummy = (djunk_sb[:, 0:1], djunk_sb[:, 1:2])
    b1_sb = _sb(esCo, "b1", 128, NE * NFF, F32)
    b2_sb = _sb(esCo, "b2", 8, EMB, BF16)
    wr_sb = _sb(esB1C, "wr", 128, NEC * NE, F32)
    wrhi_sb = _sb(esB1C, "wrhi", 128, NEC * NE, BF16)
    wrlo_sb = _sb(esB1C, "wrlo", 128, NEC * NE, BF16)
    brbc_sb = _sb(esB1C, "brbc", 128, NE, F32)
    xnfm_sb = _sb(esB1C, "xnfm", 128, NEC * S, BF16)
    maskc_sb = _sb(esB1C, "maskc", 128, NTC * NE, F32)
    maskem_sb = _sb(esB1C, "maskem", 8, S, BF16)
    mq_sb = _sb(esAB3, "mq", 128, NFC * 128, BF16)
    mk_sb = _sb(esAB3, "mk", 128, NFC * 128, BF16)
    mv_sb = _sb(esAB3, "mv", 128, NFC * 128, BF16)
    wig_sb = _sb(esAB3, "wig", 128, 48 * 8, BF16)
    wtld_sb = _sb(esAB3, "wtld", 128, NFC * 8, BF16)
    convw_sb = _sb(esB2o, "convw", 128, NFC * KS, F32)
    convb_sb = _sb(esB2o, "convb", 128, NFC, F32)

    def dchunk(dram_ap, offset, p_stride, d1_stride, d1_n, d2_n):
        return bass.AP(dram_ap.tensor, offset,
                       [[p_stride, 128], [d1_stride, d1_n], [1, d2_n]])

    with ExitStack() as es:
        pA = es.enter_context(nc.psum_tensor("pA", [128, 512], F32))
        pB = es.enter_context(nc.psum_tensor("pB", [128, 8], F32))
        mvT_tmp = es.enter_context(
            nc.sbuf_tensor("mvTt", [128, NFC * 128], BF16))

        for dst, src in ((convw_sb, convw_d), (convb_sb, convb_d),
                         (big_sb, big_d), (bfg_sb, bfg_d),
                         (b1_sb, b1_d), (lnw_p0, lnw_d), (br_p0, br_d),
                         (onesr_sb, onesr_d), (onescb_sb, onescb_d),
                         (identbf_sb, identbf_d), (ident32_sb, ident32_d),
                         (wr_sb, wr_d)):
            r.dma("sync", dst[:], src, qS)
        tokS = qS.tok()
        r.emit("vector", "memset", djunk_sb[:], 0.0)
        r.emit("vector", "memset", epscol_sb[:], LN_EPS)
        r.emit("vector", "memset", rt5eps_sb[:], RT5 * EPS)
        r.emit("vector", "memset", onecol_sb[:], 1.0, inc=sV)
        for dst, src in ((mq_sb, mq_d), (mk_sb, mk_d), (mv_sb, mv_d),
                         (mvT_tmp, mvT_d)):
            r.dma("gpsimd", dst[:], dchunk(src, 0, 128, 128 * 128, NFC, 128),
                  qG)
        r.dma("gpsimd", wig_sb[:], wigfgT_d, qG)
        r.dma("gpsimd", b2_sb[:], b2_d, qG)
        tokG = qG.tok()

        t1 = r.emit("tensor", "matmul", pB[:], onesr_sb[:], br_p0[:],
                    start=True, stop=True, wait=[tokS], inc=sT)
        tbr = r.emit("vector", "tensor_copy", brbc_sb[:], pB[:],
                     wait=[t1], inc=sV)
        tvh_ = r.emit("vector", "tensor_copy", wrhi_sb[:], wr_sb[:],
                      wait=[tokS], inc=sV)
        r.emit("vector", "scalar_tensor_tensor", wrlo_sb[:], wr_sb[:], 1.0,
               wrhi_sb[:], ALU.mult, ALU.subtract, wait=[tvh_], inc=sV)
        tprev = None
        for fc in range(NFC):
            w = [tokG, tbr] if fc == 0 else [tprev]
            t1 = r.emit("tensor", "matmul", pB[:],
                        mvT_tmp[:, fc * 128:(fc + 1) * 128],
                        wig_sb[:, (32 + fc) * 8:(33 + fc) * 8],
                        start=True, stop=True, wait=w, inc=sT)
            tprev = r.emit("scalar", "copy", wtld_sb[:, fc * 8:(fc + 1) * 8],
                           pB[:], wait=[t1], inc=sS)
        r.barrier(P, queue_waits=[("sync", qS), ("gpsimd", qG)])

    if _B1_SERIAL:
        _orig_emit = r.emit
        def _ser_emit(eng, fn, *args, wait=(), **kw):
            extra = [(s_, s_.n) for s_ in (sT, sV, sS, qS, qA, qG)]
            return _orig_emit(eng, fn, *args,
                              wait=list(wait) + extra, **kw)
        r.emit = _ser_emit
    with ExitStack() as es:
        xin = [es.enter_context(nc.sbuf_tensor(f"xin{i}", [128, EMB], F32))
               for i in range(2)]
        xsq = es.enter_context(nc.sbuf_tensor("xsq", [128, EMB], F32))
        xnt = es.enter_context(nc.sbuf_tensor("xnt", [128, EMB], F32))
        hi_tok = es.enter_context(nc.sbuf_tensor("hitok", [128, EMB], BF16))
        lo_tok = es.enter_context(nc.sbuf_tensor("lotok", [128, EMB], BF16))
        lo_fm = es.enter_context(nc.sbuf_tensor("lofm", [128, EMB], BF16))
        cols = es.enter_context(nc.sbuf_tensor("colsB1", [128, 8], F32))
        sc_sb = es.enter_context(nc.sbuf_tensor("scB1", [128, NE], F32))
        eq_sb = es.enter_context(nc.sbuf_tensor("eqB1", [128, NE], F32))
        pp_sb = es.enter_context(nc.sbuf_tensor("ppB1", [128, NE], F32))
        mbf_sb = es.enter_context(nc.sbuf_tensor("mbfB1", [128, NE], BF16))
        ptr = [es.enter_context(nc.psum_tensor(f"ptr{i}", [128, 128], BF16))
               for i in range(4)]
        psc = es.enter_context(nc.psum_tensor("psc", [128, NE], F32))
        pmk = es.enter_context(nc.psum_tensor("pmk", [8, 128], BF16))
        c_sum, c_ssq, c_m, c_msq, c_var, c_a, c_b, c_mx = [
            cols[:, i:i + 1] for i in range(8)]

        xin_free = [None, None]
        bank_free = [(), (), (), ()]
        sc_free = None
        mk_free = None
        hilo_read = None
        mbf_read = None

        for tc in range(NTC):
            b = tc % 2
            ser = [(sT, sT.n), (sV, sV.n), (sS, sS.n)] if _B1_SERIAL else []
            tl = r.dma("sync", xin[b][:],
                       bass.AP(x_c.tensor, tc * 128 * EMB,
                               [[EMB, 128], [1, EMB]]), LB1[b],
                       wait=both(xin_free[b]) + ser)
            tv = r.emit("vector", "reduce_sum", c_sum, xin[b][:], axis=AX,
                        wait=[tl], inc=sV)
            ts_ = r.emit("scalar", "activation", xsq[:], xin[b][:], AF.Square,
                         accum_out=c_ssq, wait=[tl], inc=sS)
            r.emit("scalar", "mul", c_m, c_sum, 1.0 / EMB, wait=[tv])
            r.emit("scalar", "activation", c_msq, c_m, AF.Square)
            ts2 = r.emit("scalar", "mul", c_msq, c_msq, -1.0, inc=sS)
            tv3 = r.emit("vector", "scalar_tensor_tensor", c_var, c_ssq,
                         1.0 / EMB, c_msq, ALU.mult, ALU.add,
                         wait=[ts_, ts2], inc=sV)
            ts2 = r.emit("scalar", "activation", c_a, c_var, AF.Ln,
                         bias=epscol_sb[:, 0:1], wait=[tv3], inc=sS)
            ts2 = r.emit("scalar", "activation", c_a, c_a, AF.Exp, scale=-0.5,
                         wait=[ts2], inc=sS)
            tvb = r.emit("vector", "scalar_tensor_tensor", c_b, c_m, -1.0,
                         c_a, ALU.mult, ALU.mult, wait=[ts2], inc=sV)
            tnorm = r.emit("vector", "tensor_scalar", xnt[:], xin[b][:],
                           c_a, c_b, ALU.mult, ALU.add, wait=[tvb], inc=sV)
            xin_free[b] = tnorm
            thi = r.emit("vector", "tensor_copy", hi_tok[:], xnt[:],
                         wait=both(tnorm, hilo_read), inc=sV)
            tlo = r.emit("vector", "scalar_tensor_tensor", lo_tok[:], xnt[:],
                         1.0, hi_tok[:], ALU.mult, ALU.subtract,
                         wait=[thi], inc=sV)

            sc_w = []
            for part, src_tok, src_sb in ((0, thi, hi_tok), (1, tlo, lo_tok)):
                eng = "scalar" if part == 0 else "vector"
                sem = sS if part == 0 else sV
                for ec in range(NEC):
                    bi = part * 2 + (ec % 2)
                    pbank = ptr[bi]
                    w = both(src_tok, *bank_free[bi])
                    tts = r.emit("tensor", "transpose", pbank[:],
                                 src_sb[:, ec * 128:(ec + 1) * 128],
                                 identbf_sb[:], wait=w, inc=sT)
                    if part == 0:
                        out_ap = bass.AP(xnfm_sb, ec * S + tc * 128,
                                         [[NEC * S, 128], [1, 128]])
                    else:
                        out_ap = lo_fm[:, ec * 128:(ec + 1) * 128]
                    ts1 = r.emit(eng, "tensor_copy", out_ap, pbank[:],
                                 wait=[tts], inc=sem)
                    bank_free[bi] = (ts1,)
                    if ec >= NEC - 2:
                        sc_w.append(ts1)
                    if part == 1 and ec == NEC - 1:
                        hilo_read = tts
            tmm = None
            nsc = 0
            for ec in range(NEC):
                hi_sl = bass.AP(xnfm_sb, ec * S + tc * 128,
                                [[NEC * S, 128], [1, 128]])
                lo_sl = lo_fm[:, ec * 128:(ec + 1) * 128]
                for lhsT, rsl in ((hi_sl, wrhi_sb), (hi_sl, wrlo_sb),
                                  (lo_sl, wrhi_sb)):
                    w = (sc_w + both(sc_free)) if nsc == 0 else ()
                    tmm = r.emit("tensor", "matmul", psc[:],
                                 lhsT, rsl[:, ec * NE:(ec + 1) * NE],
                                 start=(nsc == 0), stop=(nsc == 3 * NEC - 1),
                                 wait=w, inc=sT if nsc == 3 * NEC - 1 else None)
                    nsc += 1
            tv = r.emit("vector", "tensor_add", sc_sb[:], psc[:], brbc_sb[:],
                        wait=[tmm], inc=sV)
            sc_free = tv
            r.emit("vector", "reduce_max", c_mx, sc_sb[:], axis=AX)
            r.emit("vector", "tensor_scalar", pp_sb[:], sc_sb[:], c_mx, None,
                   ALU.subtract)
            r.emit("vector", "tensor_single_scalar", eq_sb[:], pp_sb[:], 0.0,
                   ALU.is_ge)
            r.emit("vector", "memset", pp_sb[:, 0:1], 0.0)
            r.emit("vector", "tensor_copy", pp_sb[:, 1:NE], eq_sb[:, 0:NE - 1])
            r.emit("vector", "tensor_copy", sc_sb[:, 0:1], pp_sb[:, 0:1])
            r.emit("vector", "tensor_add", sc_sb[:, 1:NE], pp_sb[:, 1:NE],
                   pp_sb[:, 0:NE - 1])
            r.emit("vector", "tensor_copy", pp_sb[:, 0:2], sc_sb[:, 0:2])
            r.emit("vector", "tensor_add", pp_sb[:, 2:NE], sc_sb[:, 2:NE],
                   sc_sb[:, 0:NE - 2])
            r.emit("vector", "tensor_copy", sc_sb[:, 0:4], pp_sb[:, 0:4])
            r.emit("vector", "tensor_add", sc_sb[:, 4:NE], pp_sb[:, 4:NE],
                   pp_sb[:, 0:NE - 4])
            r.emit("vector", "tensor_single_scalar", pp_sb[:], sc_sb[:], 0.5,
                   ALU.is_lt)
            tv = r.emit("vector", "tensor_mul",
                        maskc_sb[:, tc * NE:(tc + 1) * NE], eq_sb[:], pp_sb[:],
                        inc=sV)
            ts_ = r.emit("scalar", "copy", mbf_sb[:],
                         maskc_sb[:, tc * NE:(tc + 1) * NE],
                         wait=both(tv, mbf_read), inc=sS)
            tts = r.emit("tensor", "transpose", pmk[:, 0:128], mbf_sb[:],
                         identbf_sb[:], wait=both(ts_, mk_free), inc=sT)
            mbf_read = tts
            mk_free = r.emit("scalar", "tensor_copy",
                             maskem_sb[:, tc * 128:(tc + 1) * 128],
                             pmk[:, 0:128], wait=[tts], inc=sS)
        if _DEBUG_TAPS:
            r.dma("scalar", taps["xnfm"], xnfm_sb[:], qA,
                  wait=[sS.tok(), sV.tok()])
            r.dma("scalar", taps["mask"], maskc_sb[:], qA, wait=[sV.tok()])
        r.barrier(P, queue_waits=[("sync", qS), ("scalar", qA),
                                  ("sync", LB1[0]), ("sync", LB1[1])])

    with ExitStack() as es:
        wstg = [es.enter_context(nc.sbuf_tensor(f"wstg{i}", [128, 1024], F32))
                for i in range(2)]
        wbf = [es.enter_context(nc.sbuf_tensor(f"wbf{i}", [128, 1024], BF16))
               for i in range(2)]
        xme = [es.enter_context(nc.sbuf_tensor(f"xme{i}", [128, 515], F32))
               for i in range(2)]
        cacc = [es.enter_context(nc.sbuf_tensor(f"cacc{i}", [128, 512], F32))
                for i in range(2)]
        xcst = [es.enter_context(nc.sbuf_tensor(f"xcst{i}", [128, 512], F32))
                for i in range(2)]
        zst = [es.enter_context(nc.sbuf_tensor(f"zst{i}", [128, 512], F32))
               for i in range(2)]
        wp = [es.enter_context(nc.psum_tensor(f"wp{i}", [128, 512], F32))
              for i in range(8)]

        wcast = [None] * NFF
        wmmtok = [None] * NFF
        tile_free = {}
        xme_free = [None, None]
        xcst_free = [None, None]
        zst_free = [None, None]
        xm_spill = {}

        for fc in range(NFF):
            g_ = (fc % 2) * 4
            bb = fc % 2
            wld = list(wcast[fc - 2]) if fc >= 2 else []
            r.dma("sync", wstg[bb][:],
                  bass.AP(wupT.tensor, fc * 128 * 1024,
                          [[1024, 128], [1, 1024]]),
                  LW0[bb], wait=wld)
            tl = LW0[bb].tok()
            wcg = both(tl, wmmtok[fc - 2] if fc >= 2 else None)
            tv = r.emit("vector", "tensor_copy", wbf[bb][:, 0:512],
                        wstg[bb][:, 0:512], wait=wcg, inc=sV)
            ts_ = r.emit("scalar", "tensor_copy", wbf[bb][:, 512:1024],
                         wstg[bb][:, 512:1024], wait=wcg, inc=sS)
            wcast[fc] = (tv, ts_)

            w = [tv, ts_]
            if fc >= 2:
                for tb in range(NTB):
                    w.append(tile_free[(fc - 2, tb)])
            tmm = None
            for ec in range(NEC):
                for tb in range(NTB):
                    tmm = r.emit(
                        "tensor", "matmul", wp[g_ + tb][:],
                        wbf[bb][:, ec * 128:(ec + 1) * 128],
                        xnfm_sb[:, ec * S + tb * 512:ec * S + tb * 512 + 512],
                        start=(ec == 0), stop=(ec == NEC - 1),
                        wait=w if (ec == 0 and tb == 0) else (),
                        inc=sT if (ec == NEC - 1 and tb == NTB - 1) else None)
            wmmtok[fc] = tmm

            if fc < NFC:
                for tb in range(NTB):
                    xb = tb % 2
                    w_ev = both(tmm, xme_free[xb], xm_spill.get((fc, tb - 2)),
                                xm_spill.get((fc - 1, tb + 2)))
                    r.emit("vector", "tensor_copy", xme[xb][:, 3:515],
                           wp[g_ + tb][:], wait=w_ev, inc=sV)
                    if tb == 0:
                        tv = r.emit("vector", "memset", xme[xb][:, 0:3], 0.0,
                                    inc=sV)
                    else:
                        tv = r.emit("vector", "tensor_copy", xme[xb][:, 0:3],
                                    xme[1 - xb][:, 512:515], inc=sV)
                    ts_ = r.emit("scalar", "mul", cacc[xb][:],
                                 xme[xb][:, 0:512],
                                 convw_sb[:, fc * KS:fc * KS + 1],
                                 wait=both(tv, xcst_free[xb]), inc=sS)
                    tvc = ts_
                    for k in range(1, KS):
                        tvc = r.emit("vector", "scalar_tensor_tensor",
                                     cacc[xb][:], xme[xb][:, k:k + 512],
                                     convw_sb[:, fc * KS + k:fc * KS + k + 1],
                                     cacc[xb][:], ALU.mult, ALU.add,
                                     wait=[ts_] if k == 1 else (), inc=sV)
                    xme_free[xb] = tvc
                    tile_free[(fc, tb)] = tvc
                    ts_ = r.emit("scalar", "activation", xcst[xb][:],
                                 cacc[xb][:], AF.Silu,
                                 bias=convb_sb[:, fc:fc + 1],
                                 wait=[tvc], inc=sS)
                    xcst_free[xb] = r.dma(
                        "scalar", bass.AP(xc_scr.tensor,
                                          fc * 128 * S + tb * 512,
                                          [[S, 128], [1, 512]]),
                        xcst[xb][:], SXC[xb], wait=[ts_])
                    xm_spill[(fc, tb)] = r.dma(
                        "gpsimd", bass.AP(xm_scr.tensor,
                                          fc * 128 * S + tb * 512,
                                          [[S, 128], [1, 512]]),
                        xme[xb][:, 3:515], SXM[xb], wait=[tvc])
            else:
                for tb in range(NTB):
                    zb = tb % 2
                    ts_ = r.emit("scalar", "activation", zst[zb][:],
                                 wp[g_ + tb][:], AF.Silu,
                                 wait=both(tmm, zst_free[zb]), inc=sS)
                    tile_free[(fc, tb)] = ts_
                    zst_free[zb] = r.dma(
                        "scalar", bass.AP(z_scr.tensor,
                                          (fc - NFC) * 128 * S + tb * 512,
                                          [[S, 128], [1, 512]]),
                        zst[zb][:], SZ2[zb], wait=[ts_])
        r.barrier(P, queue_waits=[
            ("sync", qS), ("gpsimd", qG), ("scalar", qA),
            ("sync", LW0[0]), ("sync", LW0[1]),
            ("scalar", SXC[0]), ("scalar", SXC[1]),
            ("scalar", SZ2[0]), ("scalar", SZ2[1]),
            ("gpsimd", SXM[0]), ("gpsimd", SXM[1])])
    esB2o.close()

    with ExitStack() as es:
        xmg = [es.enter_context(
            nc.sbuf_tensor(f"xmg{i}", [128, 4 * 512], BF16)) for i in range(2)]
        xcg = [es.enter_context(
            nc.sbuf_tensor(f"xcg{i}", [128, 4 * 512], F32)) for i in range(2)]
        xcb = [es.enter_context(
            nc.sbuf_tensor(f"xcb{i}", [128, 4 * 512], BF16)) for i in range(2)]
        qbf = es.enter_context(nc.sbuf_tensor("qbf", [128, 4 * 512], BF16))
        kbf = es.enter_context(nc.sbuf_tensor("kbf", [128, 4 * 512], BF16))
        vstg = es.enter_context(nc.sbuf_tensor("vstg", [128, 4 * 512], BF16))
        pv = [es.enter_context(nc.psum_tensor(f"pv{i}", [128, 128], F32))
              for i in range(4)]
        pq = [es.enter_context(nc.psum_tensor(f"pq{i}", [128, 512], F32))
              for i in range(2)]
        pig = es.enter_context(nc.psum_tensor("pig", [4, 512], F32))
        pfg = es.enter_context(nc.psum_tensor("pfg", [4, 512], F32))
        igst = es.enter_context(nc.sbuf_tensor("igst", [4, 512], F32))
        fgst = es.enter_context(nc.sbuf_tensor("fgst", [4, 512], F32))

        ld_free = [None, None]
        pq_free = [None, None]
        pv_free = [None] * 4
        qk_spill = [None, None]
        v_spill = None
        ig_evac = None
        ig_spill = None

        for tb in range(NTB):
            tig = None
            for h in range(NH):
                i = h % 2
                tl1 = r.dma("sync", xmg[i][:],
                            dchunk(xm_scr, 4 * h * 128 * S + tb * 512,
                                   S, 128 * S, 4, 512), LB3[i],
                            wait=both(ld_free[i]))
                tl2 = r.dma("sync", xcg[i][:],
                            dchunk(xc_scr, 4 * h * 128 * S + tb * 512,
                                   S, 128 * S, 4, 512), LB3c[i])
                tvx = r.emit("vector", "tensor_copy", xcb[i][:, 0:1024],
                             xcg[i][:, 0:1024], wait=[tl2], inc=sV)
                tsx = r.emit("scalar", "tensor_copy", xcb[i][:, 1024:2048],
                             xcg[i][:, 1024:2048], wait=[tl2], inc=sS)
                qk_ev = []
                for which, msb, dstbf, scrd, spl in (
                        (0, mq_sb, qbf, q_scr, 0), (1, mk_sb, kbf, k_scr, 1)):
                    evs = [None, None]
                    for fl in range(4):
                        pb = (fl % 2)
                        fcg = 4 * h + fl
                        w = both(tvx, tsx, pq_free[pb], evs[pb])
                        tq = r.emit("tensor", "matmul", pq[pb][:],
                                    msb[:, fcg * 128:(fcg + 1) * 128],
                                    xcb[i][:, fl * 512:(fl + 1) * 512],
                                    start=True, stop=True, wait=w, inc=sT)
                        evs[pb] = r.emit(
                            "scalar", "tensor_copy",
                            dstbf[:, fl * 512:(fl + 1) * 512], pq[pb][:],
                            wait=both(tq, qk_spill[spl]), inc=sS)
                        pq_free[pb] = evs[pb]
                    qk_ev.append(evs)
                    qk_spill[spl] = r.dma(
                        "scalar", dchunk(scrd, 4 * h * 128 * S + tb * 512,
                                         S, 128 * S, 4, 512),
                        dstbf[:], SQK[spl], wait=both(*evs))
                nmm = 0
                for tcl in range(NTB):
                    for fl in range(4):
                        fcg = 4 * h + fl
                        bi = nmm % 4
                        tvm = r.emit(
                            "tensor", "matmul", pv[bi][:],
                            xmg[i][:, fl * 512 + tcl * 128:
                                   fl * 512 + tcl * 128 + 128],
                            mv_sb[:, fcg * 128:(fcg + 1) * 128],
                            start=True, stop=True,
                            wait=both(tl1 if nmm == 0 else None,
                                      pv_free[bi],
                                      v_spill if nmm == 0 else None),
                            inc=sT)
                        pv_free[bi] = r.emit(
                            "vector", "tensor_copy",
                            vstg[:, tcl * 512 + fl * 128:
                                 tcl * 512 + fl * 128 + 128],
                            pv[bi][:], wait=[tvm], inc=sV)
                        nmm += 1
                v_spill = r.dma(
                    "sync", bass.AP(v_scr.tensor, (h * S + tb * 512) * DH,
                                    [[DH, 128], [128 * DH, 4], [1, DH]]),
                    vstg[:], SV2,
                    wait=both(*[pv_free[t] for t in range(4)]))
                first = (h == 0)
                last = (h == NH - 1)
                tig = None
                for fl in range(4):
                    fcg = 4 * h + fl
                    srcs = ((wig_sb[:, fcg * 8:fcg * 8 + 4],
                             wig_sb[:, fcg * 8 + 4:fcg * 8 + 8],
                             qbf[:, fl * 512:(fl + 1) * 512]),
                            (wig_sb[:, (16 + fcg) * 8:(16 + fcg) * 8 + 4],
                             wig_sb[:, (16 + fcg) * 8 + 4:(16 + fcg) * 8 + 8],
                             kbf[:, fl * 512:(fl + 1) * 512]),
                            (wtld_sb[:, fcg * 8:fcg * 8 + 4],
                             wtld_sb[:, fcg * 8 + 4:fcg * 8 + 8],
                             xmg[i][:, fl * 512:(fl + 1) * 512]))
                    for si, (lhsT_i, lhsT_f, rhs) in enumerate(srcs):
                        st = first and fl == 0 and si == 0
                        sp = last and fl == 3 and si == 2
                        w = both(qk_ev[0][1], qk_ev[0][0],
                                 qk_ev[1][1], qk_ev[1][0],
                                 ig_evac) if (fl == 0 and si == 0) else ()
                        r.emit("tensor", "matmul", pig[:], lhsT_i, rhs,
                               start=st, stop=sp, wait=w)
                        tig = r.emit("tensor", "matmul", pfg[:], lhsT_f, rhs,
                                     start=st, stop=sp,
                                     inc=sT if sp else None)
                ld_free[i] = tig if tig is not None else sT.tok()
            w_st = both(ld_free[(NH - 1) % 2], ig_spill)
            tig1 = r.emit("vector", "tensor_scalar", igst[:], pig[:],
                          big_sb[:], None, ALU.add, wait=w_st, inc=sV)
            ig_evac = r.emit("vector", "tensor_scalar", fgst[:], pfg[:],
                             bfg_sb[:], None, ALU.add, inc=sV)
            r.dma("scalar", bass.AP(rows_scr.tensor, 16 * 8192 + tb * 512,
                                    [[8192, 4], [1, 512]]), igst[:], qA,
                  wait=[tig1])
            ig_spill = r.dma("scalar",
                             bass.AP(rows_scr.tensor, 20 * 8192 + tb * 512,
                                     [[8192, 4], [1, 512]]), fgst[:], SIG,
                             wait=[ig_evac])
        r.barrier(P, queue_waits=[
            ("sync", qS), ("scalar", qA),
            ("sync", LB3[0]), ("sync", LB3[1]),
            ("sync", LB3c[0]), ("sync", LB3c[1]),
            ("scalar", SQK[0]), ("scalar", SQK[1]),
            ("sync", SV2), ("scalar", SIG)])
    if _B1_SERIAL:
        r.emit = _orig_emit
    esAB3.close()

    with ExitStack() as es:
        h_sb = es.enter_context(
            nc.sbuf_tensor("h_sb", [128, NFF * 1024], BF16))
        w2bf = es.enter_context(
            nc.sbuf_tensor("w2bf", [128, NFF * 512], BF16))
        w1stg = [es.enter_context(
            nc.sbuf_tensor(f"w1stg{i}", [128, 1024], F32)) for i in range(2)]
        w1bf = [es.enter_context(
            nc.sbuf_tensor(f"w1bf{i}", [128, 1024], BF16)) for i in range(2)]
        w2stg = [es.enter_context(
            nc.sbuf_tensor(f"w2stg{i}", [128, 1024], F32)) for i in range(4)]
        outacc = es.enter_context(
            nc.sbuf_tensor("outacc", [128, 8 * 1024], F32))
        ph = [es.enter_context(nc.psum_tensor(f"phC{i}", [128, 512], F32))
              for i in range(4)]
        py = [es.enter_context(nc.psum_tensor(f"pyC{i}", [128, 512], F32))
              for i in range(4)]

        g_spill = None

        w1prev = [None, None]
        w1mmprev = [None, None]
        w2prev = [None] * 4
        w2_read = None
        ph_free = [None] * 4
        seg_rel = [None] * 16
        for tb2 in range(2):
            init_tok = [None] * 8
            w_init = both(w2_read, g_spill)
            for tcl in range(8):
                pb = py[tcl % 4]
                w = w_init + both(init_tok[tcl - 4] if tcl >= 4 else None)
                t2 = None
                for hb in range(2):
                    t1 = r.emit("tensor", "matmul", pb[:],
                                maskem_sb[0:8,
                                          (tb2 * 8 + tcl) * 128:
                                          (tb2 * 8 + tcl) * 128 + 128],
                                b2_sb[0:8, hb * 512:(hb + 1) * 512],
                                start=True, stop=True,
                                wait=w if hb == 0 else both(t2), inc=sT)
                    t2 = r.emit("vector", "tensor_copy",
                                outacc[:, tcl * 1024 + hb * 512:
                                       tcl * 1024 + hb * 512 + 512],
                                pb[:], wait=[t1], inc=sV)
                init_tok[tcl] = t2
            evac_done = {tcl: init_tok[tcl] for tcl in range(8)}
            pball = ph + py
            for e in range(NE):
                geltok = [None] * NFF
                for fc in range(NFF):
                    bb = fc % 2
                    wl = both(*(w1prev[bb] or ()))
                    r.dma("sync", w1stg[bb][:],
                          bass.AP(w1_d.tensor,
                                  (e * 32 + fc) * 128 * 1024,
                                  [[1024, 128], [1, 1024]]),
                          LW1[bb], wait=wl)
                    tl = LW1[bb].tok()
                    wcg = both(tl, w1mmprev[bb])
                    tvw = r.emit("vector", "tensor_copy", w1bf[bb][:, 0:1024],
                                 w1stg[bb][:, 0:1024], wait=wcg, inc=sV)
                    w1prev[bb] = (tvw,)
                    gp = (fc % 2) * 2
                    tmm = None
                    for ec in range(NEC):
                        for th in range(2):
                            tmm = r.emit(
                                "tensor", "matmul", ph[gp + th][:],
                                w1bf[bb][:, ec * 128:(ec + 1) * 128],
                                xnfm_sb[:, ec * S + tb2 * 1024 + th * 512:
                                        ec * S + tb2 * 1024 + th * 512 + 512],
                                start=(ec == 0), stop=(ec == NEC - 1),
                                wait=both(tvw, ph_free[gp],
                                          ph_free[gp + 1],
                                          w2_read if fc < 2 else None,
                                          *([evac_done[t] for t in range(4)]
                                            if fc < 2 else ()))
                                if (ec == 0 and th == 0) else (),
                                inc=sT if (ec == NEC - 1 and th == 1) else None)
                    w1mmprev[bb] = tmm
                    for th in range(2):
                        ph_free[gp + th] = r.emit(
                            "scalar", "activation",
                            h_sb[:, fc * 1024 + th * 512:
                                 fc * 1024 + th * 512 + 512],
                            ph[gp + th][:], AF.Gelu,
                            bias=b1_sb[:, e * NFF + fc:e * NFF + fc + 1],
                            wait=[tmm], inc=sS)
                    geltok[fc] = (ph_free[gp], ph_free[gp + 1])
                for hb in range(2):
                    w2casts = [None] * 16
                    for cs in range(16):
                        bb = cs % 4
                        src = w2casts[cs - 4] if cs >= 4 else w2prev[bb]
                        wl = list(src) if src else []
                        r.dma("sync", w2stg[bb][:],
                              bass.AP(w2_d.tensor,
                                      ((e * 2 + hb) * 16 + cs) * 128 * 1024,
                                      [[1024, 128], [1, 1024]]),
                              LW2[bb], wait=wl)
                        tl = LW2[bb].tok()
                        wc = both(tl, seg_rel[cs])
                        tvw = r.emit("vector", "tensor_copy",
                                     w2bf[:, cs * 1024:cs * 1024 + 512],
                                     w2stg[bb][:, 0:512], wait=wc, inc=sV)
                        tsw = r.emit("scalar", "tensor_copy",
                                     w2bf[:, cs * 1024 + 512:cs * 1024 + 1024],
                                     w2stg[bb][:, 512:1024], wait=wc,
                                     inc=sS)
                        w2casts[cs] = (tvw, tsw)
                        w2prev[bb] = (tvw, tsw)
                    tmm = None
                    for fc in range(NFF):
                        for tcl in range(8):
                            w = ()
                            if fc == 0:
                                w = both(ph_free[tcl] if tcl < 4 else None,
                                         evac_done[tcl],
                                         *(w2casts[0] if tcl == 0 else ()),
                                         geltok[0][tcl // 4])
                            elif tcl == 0:
                                w = both(geltok[fc][0],
                                         *(w2casts[fc // 2]
                                           if fc % 2 == 0 else ()))
                            elif tcl == 4:
                                w = both(geltok[fc][1])
                            rel = (fc % 2 == 1 and tcl == 7)
                            tmm = r.emit(
                                "tensor", "matmul", pball[tcl][:],
                                h_sb[:, fc * 1024 + tcl * 128:
                                     fc * 1024 + tcl * 128 + 128],
                                w2bf[:, fc * 512:(fc + 1) * 512],
                                start=(fc == 0), stop=(fc == NFF - 1),
                                wait=w, inc=sT if rel else None)
                            if rel:
                                seg_rel[fc // 2] = (sT, sT.n)
                    w2_read = tmm
                    for tcl in range(8):
                        col = tcl * 1024 + hb * 512
                        evac_done[tcl] = r.emit(
                            "vector", "scalar_tensor_tensor",
                            outacc[:, col:col + 512], pball[tcl][:],
                            maskc_sb[:, (tb2 * 8 + tcl) * NE + e:
                                     (tb2 * 8 + tcl) * NE + e + 1],
                            outacc[:, col:col + 512],
                            ALU.mult, ALU.add, wait=[tmm], inc=sV)
            last_ev = [evac_done[t] for t in range(8)]
            if _DEBUG_TAPS:
                r.dma("scalar", bass.AP(taps["ffn"].tensor,
                                        tb2 * 1024 * EMB,
                                        [[EMB, 128], [128 * EMB, 8], [1, EMB]]),
                      outacc[:], qA, wait=both(*last_ev))
            tsil = r.emit("scalar", "activation", outacc[:], outacc[:],
                          AF.Silu, wait=both(*last_ev), inc=sS)
            g_spill = r.dma("gpsimd",
                            bass.AP(g_scr.tensor, tb2 * 1024 * EMB,
                                    [[EMB, 128], [128 * EMB, 8], [1, EMB]]),
                            outacc[:], SG2, wait=[tsil])
        r.barrier(P, queue_waits=[
            ("sync", qS), ("gpsimd", qG), ("scalar", qA),
            ("sync", LW1[0]), ("sync", LW1[1]),
            ("sync", LW2[0]), ("sync", LW2[1]), ("sync", LW2[2]),
            ("sync", LW2[3]), ("gpsimd", SG2)])
    esB1C.close()
    esCo.close()

    LQ = [r.sem(f"LQ{i}", "sync") for i in range(2)]
    LK = [r.sem(f"LK{i}", "sync") for i in range(2)]
    LV = [r.sem(f"LV{i}", "sync") for i in range(2)]
    LX = r.sem("LX", "sync")
    LZ = r.sem("LZ", "sync")
    LC = r.sem("LC", "sync")
    LE = r.sem("LE", "sync")
    LG = r.sem("LG", "sync")
    LO = r.sem("LO", "sync")
    LR = r.sem("LR", "sync")
    NS = r.sem("NS", "scalar")
    OS = r.sem("OS", "sync")
    WD = r.sem("WD", "gpsimd")
    DSEMS = (sT, sV, sS, qS, qA, qG, LQ[0], LQ[1], LK[0], LK[1],
             LV[0], LV[1], LX, LZ, LC, LE, LG, LO, LR, NS, OS, WD)
    ser_tags = set(_D_ALL_TAGS) if _D_SERIAL else set(_D_SER_TAGS)
    _dtag = [None]

    def dser(tag):
        _dtag[0] = tag
    if ser_tags:
        _orig_emitD = r.emit

        def _ser_emitD(eng, fn, *args, wait=(), **kw):
            if _dtag[0] in ser_tags:
                extra = [(s_, s_.n) for s_ in DSEMS]
                wait = list(wait) + extra
            return _orig_emitD(eng, fn, *args, wait=wait, **kw)
        r.emit = _ser_emitD
    with ExitStack() as es:
        posm_sb = es.enter_context(nc.sbuf_tensor("posmD", [128, 4 * 512], F32))
        lnbc_sb = es.enter_context(nc.sbuf_tensor("lnbcD", [128, NH * DH], F32))
        skipc_sb = es.enter_context(nc.sbuf_tensor("skipcD", [128, NFC], F32))
        ucol = es.enter_context(nc.sbuf_tensor("ucol", [128, 64], F32))
        dser("pre")
        tpo = r.dma("sync", posm_sb[:],
                    dchunk(posm_d, 0, 512, 128 * 512, 4, 512), LR)
        tsk = r.dma("sync", skipc_sb[:], skipc_d, LR)
        with ExitStack() as esrows:
            ig_sb = esrows.enter_context(nc.sbuf_tensor("ig4", [4, S], F32))
            fg_sb = esrows.enter_context(nc.sbuf_tensor("fg4", [4, S], F32))
            tgld = r.dma("sync", ig_sb[:],
                         bass.AP(rows_scr.tensor, 16 * 8192,
                                 [[8192, 4], [1, S]]), LR)
            tgld2 = r.dma("sync", fg_sb[:],
                          bass.AP(rows_scr.tensor, 20 * 8192,
                                  [[8192, 4], [1, S]]), LR)
            rt1 = esrows.enter_context(nc.sbuf_tensor("rt1", [4, S], F32))
            rt2 = esrows.enter_context(nc.sbuf_tensor("rt2", [4, S], F32))
            csr = esrows.enter_context(nc.sbuf_tensor("csr", [4, S], F32))
            urow = esrows.enter_context(nc.sbuf_tensor("urow", [4, S], F32))
            cmrow = esrows.enter_context(nc.sbuf_tensor("cmrow", [4, S], F32))
            enrow = esrows.enter_context(nc.sbuf_tensor("enrow", [4, S], F32))
            ts_ = r.emit("scalar", "activation", rt1[:], fg_sb[:],
                         AF.Abs, wait=[tgld2], inc=sS)
            ts_ = r.emit("scalar", "activation", rt1[:], rt1[:], AF.Exp,
                         scale=-1.0, wait=[ts_], inc=sS)
            ts_ = r.emit("scalar", "activation", rt1[:], rt1[:], AF.Ln,
                         bias=onecol_sb[0:4, 0:1], wait=[ts_], inc=sS)
            r.emit("vector", "tensor_single_scalar", rt2[:], fg_sb[:],
                   0.0, ALU.min, wait=[tgld2])
            tv = r.emit("vector", "tensor_sub", rt2[:], rt2[:], rt1[:],
                        wait=[ts_], inc=sV)
            tv = r.emit("vector", "tensor_tensor_scan", csr[:], rt2[:],
                        rt2[:], 0.0, ALU.add, ALU.bypass, wait=[tv], inc=sV)
            tv = r.emit("vector", "tensor_sub", urow[:], ig_sb[:],
                        csr[:], wait=both(tv, tgld), inc=sV)
            tvcm = r.emit("vector", "tensor_tensor_scan", cmrow[:], urow[:],
                          urow[:], -3.0e38, ALU.max, ALU.max, wait=[tv],
                          inc=sV)
            tv2 = r.emit("vector", "tensor_add", rt1[:], csr[:], cmrow[:],
                         wait=[tvcm], inc=sV)
            tsen = r.emit("scalar", "activation", enrow[:], rt1[:], AF.Exp,
                          scale=-1.0, wait=[tv2], inc=sS)
            t1 = r.dma("scalar", bass.AP(rows_scr.tensor, 0,
                                         [[8192, 4], [1, S]]), cmrow[:], NS,
                       wait=[tvcm])
            t2 = r.dma("scalar", bass.AP(rows_scr.tensor, 4 * 8192,
                                         [[8192, 4], [1, S]]), urow[:], NS,
                       wait=[tv])
            t3 = r.dma("scalar", bass.AP(rows_scr.tensor, 8 * 8192,
                                         [[8192, 4], [1, S]]), enrow[:], NS,
                       wait=[tsen])
            with nc.psum_tensor("pucol", [128, 4], F32) as pu:
                tprev_u = None
                for ti_ in range(16):
                    tt_ = r.emit("tensor", "transpose", pu[:],
                                 urow[:, ti_ * 128:(ti_ + 1) * 128],
                                 ident32_sb[0:4, 0:4],
                                 wait=both(tv, tprev_u), inc=sT)
                    tprev_u = r.emit(
                        "vector", "tensor_copy",
                        bass.AP(ucol, ti_, [[64, 128], [16, 4], [1, 1]]),
                        pu[:], wait=[tt_], inc=sV)
                tld_rows = tprev_u
            tcm_rows = t1
            ten_rows = t3
            trowsq = (NS, NS.n)

        cmp0 = es.enter_context(nc.sbuf_tensor("cmp0", [1, 512], F32))
        enp0 = es.enter_context(nc.sbuf_tensor("enp0", [1, 512], F32))
        q_sb = [es.enter_context(
            nc.sbuf_tensor(f"qD{i}", [128, 4 * 512], BF16)) for i in range(2)]
        k_sb = [es.enter_context(
            nc.sbuf_tensor(f"kD{i}", [128, 4 * 2048], BF16)) for i in range(2)]
        v_sb = [es.enter_context(
            nc.sbuf_tensor(f"vD{i}", [128, 16 * 512], BF16)) for i in range(2)]
        xcD_ = es.enter_context(nc.sbuf_tensor("xcD", [128, 4 * 512], F32))
        zD_ = es.enter_context(nc.sbuf_tensor("zD", [128, 4 * 512], F32))
        wdD_ = es.enter_context(nc.sbuf_tensor("wdD", [128, 4 * 1024], BF16))
        xcD = [xcD_, xcD_]
        zD = [zD_, zD_]
        wdD = [wdD_, wdD_]
        cmbc = es.enter_context(nc.sbuf_tensor("cmbc", [128, 512], F32))
        dex = [es.enter_context(
            nc.sbuf_tensor(f"dex{i}", [128, 512], F32)) for i in range(2)]
        dtmp_ = es.enter_context(nc.sbuf_tensor("dtmp", [128, 512], F32))
        dtmp = [dtmp_, dtmp_]
        cbf = [es.enter_context(
            nc.sbuf_tensor(f"cbf{i}", [128, 512], BF16)) for i in range(2)]
        hts = [es.enter_context(
            nc.sbuf_tensor(f"hts{i}", [128, 512], F32)) for i in range(4)]
        hsqs = es.enter_context(nc.sbuf_tensor("hsqs", [128, 512], F32))
        hfm = es.enter_context(nc.sbuf_tensor("hfm", [128, 512], F32))
        htmp = es.enter_context(nc.sbuf_tensor("htmp", [128, 512], F32))
        hstb = es.enter_context(nc.sbuf_tensor("hstb", [128, 4 * 512], BF16))
        nrow = es.enter_context(nc.sbuf_tensor("nrow", [1, 512], F32))
        nu4 = es.enter_context(nc.sbuf_tensor("nu4", [4, 128], F32))
        nucol = es.enter_context(nc.sbuf_tensor("nucol", [128, 4], F32))
        dcol = es.enter_context(nc.sbuf_tensor("dcolD", [128, 16], F32))
        dcolB = es.enter_context(nc.sbuf_tensor("dcolB", [128, 8], F32))
        g_sbD = es.enter_context(nc.sbuf_tensor("gD", [128, 4 * 1024], BF16))
        out_sb = es.enter_context(nc.sbuf_tensor("outD", [128, 4 * 1024], F32))
        s_sb = es.enter_context(nc.sbuf_tensor("sD", [128, 4 * 1024], F32))
        stmp = es.enter_context(nc.sbuf_tensor("stmpD", [128, 512], F32))
        pst = [es.enter_context(nc.psum_tensor(f"pstD{i}", [128, 512], F32))
               for i in range(2)]
        ph4 = [es.enter_context(nc.psum_tensor(f"ph4D{i}", [128, 512], F32))
               for i in range(4)]

        tlnb = None
        with nc.psum_tensor("plnb", [128, 512], F32) as plnb:
            for h_ in range(NH):
                t1 = r.emit("tensor", "matmul", plnb[:], onesr_sb[:],
                            lnw_p0[0:1, h_ * DH:(h_ + 1) * DH],
                            start=True, stop=True,
                            wait=both(tlnb), inc=sT)
                tlnb = r.emit("vector", "tensor_copy",
                              lnbc_sb[:, h_ * DH:(h_ + 1) * DH], plnb[:],
                              wait=[t1], inc=sV)

        ld_free = [None, None]
        xcz_free = None
        wd_free = None
        cm_row_free = None
        en_row_free = None
        dtmp_free = None
        pst_free = [None, None]
        prs_free = None
        pcm_free = None
        ph4_free = [None] * 4
        prs_free = None
        pcm_free = None
        cm_used = None
        dex_free = [None, None]
        sj_done = None
        out_dma = None

        for sj in range(NTB):
            NT = 4 * (sj + 1)
            T = 512 * (sj + 1)
            dser("ld")
            tgl = r.dma("sync", g_sbD[:],
                        bass.AP(g_scr.tensor, sj * 512 * EMB,
                                [[EMB, 128], [128 * EMB, 4], [1, EMB]]), LG,
                        wait=both(sj_done, trowsq if sj == 0 else None))
            tol = r.dma("sync", out_sb[:],
                        bass.AP(x_c.tensor, sj * 512 * EMB,
                                [[EMB, 128], [128 * EMB, 4], [1, EMB]]), LO,
                        wait=both(out_dma))
            s_init = [False] * 8
            for h in range(NH):
                idx = sj * 4 + h
                i = idx % 2
                wl = both(ld_free[i], tld_rows if idx == 0 else None,
                          trowsq if idx == 0 else None)
                dser("ld")
                tq = r.dma("sync", q_sb[i][:],
                           dchunk(q_scr, 4 * h * 128 * S + sj * 512,
                                  S, 128 * S, 4, 512), LQ[i], wait=wl)
                tk = r.dma("sync",
                           bass.AP(k_sb[i], 0, [[4 * 2048, 128], [T, 4], [1, T]]),
                           dchunk(k_scr, 4 * h * 128 * S, S, 128 * S, 4, T),
                           LK[i], wait=wl)
                tvl = r.dma("sync",
                            bass.AP(v_sb[i], 0,
                                    [[16 * 512, 128], [512, NT], [1, 512]]),
                            bass.AP(v_scr.tensor, h * S * DH,
                                    [[DH, 128], [128 * DH, NT], [1, DH]]), LV[i],
                            wait=wl)
                txc = r.dma("sync", xcD[i][:],
                            dchunk(xc_scr, 4 * h * 128 * S + sj * 512,
                                   S, 128 * S, 4, 512), LX,
                            wait=both(xcz_free))
                tz = r.dma("sync", zD[i][:],
                           dchunk(z_scr, 4 * h * 128 * S + sj * 512,
                                  S, 128 * S, 4, 512), LZ,
                           wait=both(xcz_free))
                tcml = r.dma("sync", cmp0[:],
                             bass.AP(rows_scr.tensor, h * 8192 + sj * 512,
                                     [[1, 1], [1, 512]]), LC,
                             wait=both(cm_row_free,
                                       tcm_rows if idx == 0 else None))
                tenl = r.dma("sync", enp0[:],
                             bass.AP(rows_scr.tensor,
                                     (8 + h) * 8192 + sj * 512,
                                     [[1, 1], [1, 512]]), LE,
                             wait=both(en_row_free,
                                       ten_rows if idx == 0 else None))
                twd = r.dma("gpsimd", wdD[i][:],
                            dchunk(wdT_d, 4 * h * 128 * EMB,
                                   EMB, 128 * EMB, 4, EMB), WD,
                            wait=both(wd_free))
                dser("cm")
                with nc.psum_tensor(f"pcm{idx}", [128, 512], F32) as pcm:
                    tcmm = r.emit("tensor", "matmul", pcm[:], onesr_sb[:],
                                  cmp0[0:1, 0:512],
                                  start=True, stop=True,
                                  wait=both(cm_used, tcml,
                                            tlnb if idx == 0 else None),
                                  inc=sT)
                    cm_row_free = tcmm
                    tcmc = r.emit("vector", "tensor_scalar", cmbc[:], pcm[:],
                                  1.0, LNRTDH, ALU.mult, ALU.add,
                                  wait=[tcmm], inc=sV)
                prs_cm = nc.psum_tensor(f"prs{idx}", [1, 512], F32)
                prs = prs_cm.__enter__()
                tvc_hist = [None, None]
                trs = None
                thm = None
                dser("ti")
                for ti in range(NT):
                    bb = ti % 2
                    rdiag = ti - 4 * sj
                    if rdiag >= 0:
                        tvp = r.emit("vector", "tensor_add", dtmp[bb][:],
                                     cmbc[:],
                                     posm_sb[:, rdiag * 512:(rdiag + 1) * 512],
                                     wait=both(tcmc, dex_free[bb], dtmp_free,
                                               tpo if idx == 0 else None),
                                     inc=sV)
                        din = dtmp[bb][:]
                        wexp = [tvp]
                    else:
                        din = cmbc[:]
                        wexp = both(tcmc, dex_free[bb])
                    tsd = r.emit("scalar", "activation", dex[bb][:], din,
                                 AF.Exp, scale=-1.0,
                                 bias=ucol[:, h * 16 + ti:h * 16 + ti + 1],
                                 wait=wexp, inc=sS)
                    if rdiag >= 0:
                        dtmp_free = tsd
                    tmm = None
                    for dc in range(4):
                        w = both(tq, tk, tvc_hist[bb], pst_free[bb]) \
                            if dc == 0 else ()
                        tmm = r.emit("tensor", "matmul", pst[bb][:],
                                     k_sb[i][:, dc * T + ti * 128:
                                             dc * T + ti * 128 + 128],
                                     q_sb[i][:, dc * 512:(dc + 1) * 512],
                                     start=(dc == 0), stop=(dc == 3),
                                     wait=w, inc=sT if dc == 3 else None)
                    tvc = r.emit("vector", "tensor_mul", cbf[bb][:],
                                 pst[bb][:], dex[bb][:],
                                 wait=both(tmm, tsd), inc=sV)
                    tvc_hist[bb] = tvc
                    pst_free[bb] = tvc
                    dex_free[bb] = tvc
                    w = both(tvc, tvl,
                             tcmc if ti == 0 else None)
                    trs = r.emit("tensor", "matmul", prs[:], onescb_sb[:],
                                 cbf[bb][:], start=(ti == 0),
                                 stop=(ti == NT - 1), wait=w,
                                 inc=sT if ti == NT - 1 else None)
                    for sc in range(4):
                        w = both(ph4_free[sc]) if ti == 0 else ()
                        thm = r.emit("tensor", "matmul", ph4[sc][:],
                                     cbf[bb][:, sc * 128:(sc + 1) * 128],
                                     v_sb[i][:, ti * 512:(ti + 1) * 512],
                                     start=(ti == 0), stop=(ti == NT - 1),
                                     wait=w,
                                     inc=sT if (ti == NT - 1 and sc == 3)
                                     else None)
                dser("nrm")
                tsa = r.emit("scalar", "activation", nrow[:], prs[:], AF.Abs,
                             wait=[trs], inc=sS)
                prs_cm.__exit__(None, None, None)
                tvn = r.emit("vector", "tensor_max", nrow[:], nrow[:],
                             enp0[0:1, 0:512],
                             wait=both(tsa, tenl), inc=sV)
                en_row_free = tvn
                tss = r.emit("scalar", "activation", nrow[:], nrow[:],
                             AF.Square, scale=RT5,
                             bias=rt5eps_sb[0:1, 0:1],
                             wait=[tvn], inc=sS)
                tnu1 = r.dma("scalar", bass.AP(rows_scr.tensor, 12 * 8192,
                                               [[512, 1], [1, 512]]),
                             nrow[:], NS, wait=[tss])
                tnu1b = r.dma("sync", nu4[:],
                              bass.AP(rows_scr.tensor, 12 * 8192,
                                      [[128, 4], [1, 128]]), LR, wait=[tnu1])
                with nc.psum_tensor(f"pnu{idx}", [128, 4], F32) as pnu:
                    ttnu = r.emit("tensor", "transpose", pnu[:], nu4[:],
                                  ident32_sb[0:4, 0:4],
                                  wait=both(tnu1b, tsa), inc=sT)
                    tnu2 = r.emit("vector", "tensor_copy", nucol[:],
                                  pnu[:], wait=[ttnu], inc=sV)
                tr_toks = []
                hn_done = [None] * 4
                dser("st")
                tvh4 = [None] * 4
                tsh4 = [None] * 4
                for sc in range(4):
                    tvh4[sc] = r.emit("vector", "tensor_scalar", hts[sc][:],
                                      ph4[sc][:], 1.0, None, ALU.mult, ALU.add,
                                      accum_out=dcol[:, sc:sc + 1],
                                      wait=[thm], inc=sV)
                    tsh4[sc] = r.emit("scalar", "activation", hsqs[:],
                                      hts[sc][:], AF.Square,
                                      accum_out=dcol[:, 4 + sc:5 + sc],
                                      wait=[tvh4[sc]], inc=sS)
                m4 = dcol[:, 8:12]
                v4 = dcol[:, 12:16]
                t4 = dcolB[:, 0:4]
                b4 = dcolB[:, 4:8]
                r.emit("scalar", "mul", m4, dcol[:, 0:4], 1.0 / DH,
                       wait=[tvh4[3]])
                r.emit("scalar", "activation", t4, m4, AF.Square)
                tss2 = r.emit("scalar", "mul", t4, t4, -1.0, inc=sS)
                tvv = r.emit("vector", "scalar_tensor_tensor", v4,
                             dcol[:, 4:8], 1.0 / DH, t4,
                             ALU.mult, ALU.add,
                             wait=both(tsh4[3], tss2), inc=sV)
                tvv = r.emit("vector", "tensor_add", v4, v4,
                             nucol[:, 0:4], wait=both(tvv, tnu2), inc=sV)
                tsv = r.emit("scalar", "activation", v4, v4, AF.Ln,
                             wait=[tvv], inc=sS)
                tsv = r.emit("scalar", "activation", v4, v4, AF.Exp,
                             scale=-0.5, wait=[tsv], inc=sS)
                tvb2 = r.emit("vector", "scalar_tensor_tensor", b4,
                              m4, -1.0, v4, ALU.mult, ALU.mult,
                              wait=[tsv], inc=sV)
                for sc in range(4):
                    tvn2 = r.emit("vector", "tensor_scalar", hts[sc][:],
                                  hts[sc][:], dcol[:, 12 + sc:13 + sc],
                                  dcolB[:, 4 + sc:5 + sc],
                                  ALU.mult, ALU.add, wait=[tvb2], inc=sV)
                    hn_done[sc] = r.emit(
                        "vector", "tensor_mul", hts[sc][:], hts[sc][:],
                        lnbc_sb[:, h * DH:(h + 1) * DH],
                        wait=[tvn2], inc=sV)
                dser("tr")
                hst_done = [None] * 4
                ptrD_cm = [nc.psum_tensor(f"ptrD{idx}_{i}", [128, 128], F32)
                           for i in range(2)]
                ptrD = [cm_.__enter__() for cm_ in ptrD_cm]
                ptrD_free = [tnu2, None]
                for dc in range(4):
                    tve = None
                    for sc in range(4):
                        bi = sc % 2
                        w = both(hn_done[0], hn_done[1], hn_done[2],
                                 hn_done[3]) if (dc == 0 and sc == 0) else []
                        tts = r.emit("tensor", "transpose", ptrD[bi][:],
                                     hts[sc][:, dc * 128:(dc + 1) * 128],
                                     ident32_sb[:],
                                     wait=list(w) + both(ptrD_free[bi]),
                                     inc=sT)
                        tve = r.emit("vector", "tensor_copy",
                                     hfm[:, sc * 128:(sc + 1) * 128],
                                     ptrD[bi][:], wait=[tts], inc=sV)
                        ptrD_free[bi] = tve
                    tvk = r.emit("vector", "scalar_tensor_tensor", htmp[:],
                                 xcD[i][:, dc * 512:(dc + 1) * 512],
                                 skipc_sb[:, 4 * h + dc:4 * h + dc + 1],
                                 hfm[:], ALU.mult, ALU.add,
                                 wait=both(tve, txc,
                                           tsk if idx == 0 else None), inc=sV)
                    hst_done[dc] = r.emit(
                        "vector", "tensor_mul",
                        hstb[:, dc * 512:(dc + 1) * 512], htmp[:],
                        zD[i][:, dc * 512:(dc + 1) * 512],
                        wait=both(tvk, tz), inc=sV)
                for cm_ in reversed(ptrD_cm):
                    cm_.__exit__(None, None, None)
                if _DEBUG_TAPS:
                    r.dma("scalar", bass.AP(taps["hst"].tensor,
                                            idx * 128 * INNER,
                                            [[INNER, 128], [1, INNER]]),
                          hstb[:], qA, wait=both(*hst_done))
                    r.dma("scalar", bass.AP(taps["hts"].tensor,
                                            idx * 128 * INNER,
                                            [[INNER, 128], [512, 4], [1, 512]]),
                          bass.AP(hts[0], 0, [[512, 128], [1, 512]]), qA,
                          wait=both(*hn_done)) if False else None
                dser("wd")
                sadd = None
                for ob in range(2):
                    for sc in range(4):
                        tmm = None
                        for dc in range(4):
                            w = both(hst_done[0], hst_done[1], hst_done[2],
                                     hst_done[3], twd,
                                     ph4_free[sc]) if dc == 0 else ()
                            tmm = r.emit(
                                "tensor", "matmul", ph4[sc][:],
                                hstb[:, dc * 512 + sc * 128:
                                     dc * 512 + sc * 128 + 128],
                                wdD[i][:, dc * EMB + ob * 512:
                                       dc * EMB + ob * 512 + 512],
                                start=(dc == 0), stop=(dc == 3),
                                wait=w, inc=sT if dc == 3 else None)
                        col = sc * 1024 + ob * 512
                        if h == 0:
                            sadd = r.emit("vector", "tensor_copy",
                                          s_sb[:, col:col + 512], ph4[sc][:],
                                          wait=both(tmm, sj_done), inc=sV)
                        else:
                            sadd = r.emit("vector", "tensor_add",
                                          s_sb[:, col:col + 512],
                                          s_sb[:, col:col + 512], ph4[sc][:],
                                          wait=[tmm], inc=sV)
                        ph4_free[sc] = sadd
                ld_free[i] = sadd
                cm_used = sadd
                xcz_free = hst_done[3]
                wd_free = tmm
            dser("cmb")
            toks = []
            if _DEBUG_TAPS:
                r.dma("scalar", bass.AP(taps["s"].tensor, sj * 512 * EMB,
                                        [[EMB, 128], [128 * EMB, 4], [1, EMB]]),
                      s_sb[:], qA, wait=[sV.tok()])
            for tile in range(8):
                col = tile * 512
                tvm = r.emit("vector", "tensor_mul", stmp[:],
                             s_sb[:, col:col + 512], g_sbD[:, col:col + 512],
                             wait=both(sV.tok() if tile == 0 else None, tgl),
                             inc=sV)
                tva = r.emit("vector", "tensor_add", out_sb[:, col:col + 512],
                             out_sb[:, col:col + 512], stmp[:],
                             wait=both(tvm, tol), inc=sV)
                toks.append(tva)
            sj_done = toks[-1]
            out_dma = r.dma("sync",
                            bass.AP(out_d.tensor, sj * 512 * EMB,
                                    [[EMB, 128], [128 * EMB, 4], [1, EMB]]),
                            out_sb[:], OS, wait=both(*toks))
        r.barrier(P, queue_waits=[
            ("sync", qS), ("gpsimd", qG), ("scalar", qA),
            ("sync", LQ[0]), ("sync", LQ[1]), ("sync", LK[0]),
            ("sync", LK[1]), ("sync", LV[0]), ("sync", LV[1]),
            ("sync", LX), ("sync", LZ), ("sync", LC), ("sync", LE),
            ("sync", LG), ("sync", LO), ("sync", LR), ("scalar", NS),
            ("sync", OS), ("gpsimd", WD)])

    esB3D.close()
    esG.close()

    global _LAST_REC
    _LAST_REC = r
    sem_stack = ExitStack()
    for s in r.sems:
        s.h = sem_stack.enter_context(nc.semaphore(s.name))
    r.replay(nc, upto=upto, cut=cut)
    return nc



def kernel(**inputs):
    shared, shards = _prep(inputs)
    nc = build_nc()
    in_maps = [dict(shared, x_c=shards[c]) for c in range(8)]
    res = run_bass_kernel_spmd(nc, in_maps, core_ids=list(range(8)))
    out = np.stack([np.asarray(res.results[c]["out"]) for c in range(8)], 0)
    return out.astype(np.float32, copy=False)
